# version 05
# baseline (speedup 1.0000x reference)
import math
from contextlib import ExitStack

import numpy as np
import ml_dtypes
import concourse.bass as bass
import concourse.mybir as mybir
from concourse.bass_utils import run_bass_kernel_spmd

BF16 = ml_dtypes.bfloat16
F32 = mybir.dt.float32
BF = mybir.dt.bfloat16
AF = mybir.ActivationFunctionType
ALU = mybir.AluOpType
AX = mybir.AxisListType

T = 2048
D = 2048
NCORE = 8

COMPUTE_ENGINES = ("tensor", "vector", "scalar", "gpsimd")
DMA_QUEUES = ("dsync", "dgps", "dscal")
QUEUE_OWNER = {"dsync": "sync", "dgps": "gpsimd", "dscal": "scalar"}


def _ap_region(ap):
    sp = str(ap.space)
    name = ap.tensor.name
    if "SBUF" in sp or "PSUM" in sp:
        try:
            p0 = ap.base_partition()
            return (name, p0, p0 + ap.partition_size())
        except Exception:
            return (name, 0, 128)
    return (name, 0, 1 << 20)


class K:

    def __init__(self, name):
        self.name = name
        self.nc = bass.Bass("TRN2", target_bir_lowering=False)
        self.es = ExitStack()
        self.records = []
        self.cnt = {e: 0 for e in COMPUTE_ENGINES + DMA_QUEUES}
        self.hist = {}
        self.bar = {}
        self.in_specs = {}
        self.out_names = []

    def din(self, name, shape, dt):
        self.in_specs[name] = (tuple(shape), dt)
        return self.nc.dram_tensor(name, list(shape), dt, kind="ExternalInput").ap()

    def dout(self, name, shape, dt):
        self.out_names.append(name)
        return self.nc.dram_tensor(name, list(shape), dt, kind="ExternalOutput").ap()

    def dtmp(self, name, shape, dt):
        return self.nc.dram_tensor(name, list(shape), dt).ap()

    def _uname(self, name):
        if not hasattr(self, "_names"):
            self._names = {}
        n = self._names.get(name, 0)
        self._names[name] = n + 1
        return name if n == 0 else f"{name}_{n}"

    def sb(self, name, shape, dt):
        return self.es.enter_context(
            self.nc.sbuf_tensor(self._uname(name), list(shape), dt))

    def ps(self, name, shape, dt=F32):
        return self.es.enter_context(
            self.nc.psum_tensor(self._uname(name), list(shape), dt))

    def scope(self):
        outer = self
        class _S:
            def __enter__(self_s):
                self_s.es = ExitStack()
                self_s.saved = outer.es
                outer.es = self_s.es
                return self_s
            def __exit__(self_s, *a):
                outer.es = self_s.saved
                self_s.es.close()
                outer.bar = dict(outer.cnt)
                return False
        return _S()

    SERIAL = False
    GLOBAL_SEQ = False
    DMA_SERIAL = True

    def op(self, eng, fn, reads=(), writes=(), inc=None, tag="", extra_waits=None,
           body=None, sem=None):
        body = body or QUEUE_OWNER.get(eng, eng)
        sem = sem or eng
        if inc is None:
            inc = 16 if eng in DMA_QUEUES else 1
        if K.SERIAL:
            waits = {e: v for e, v in self.cnt.items() if v > 0}
            if eng not in DMA_QUEUES:
                waits.pop(sem, None)
        elif K.GLOBAL_SEQ:
            waits = {}
        else:
            waits = {e: v for e, v in self.bar.items()
                     if v > 0 and (e != sem or eng in DMA_QUEUES)}
        def dep(region, writes_only):
            nm, p0, p1 = region
            for iw, q0, q1, e, c in self.hist.get(nm, ()):
                if q1 <= p0 or q0 >= p1 or (writes_only and not iw):
                    continue
                if e == sem:
                    if sem == "tensor":
                        continue
                    if sem in COMPUTE_ENGINES and self.cnt.get(sem, 0) - c >= 24:
                        continue
                if waits.get(e, 0) < c:
                    waits[e] = c
        rregs = [_ap_region(a) for a in reads]
        wregs = [_ap_region(a) for a in writes]
        if not (K.SERIAL or K.GLOBAL_SEQ):
            for rg in rregs:
                dep(rg, True)
            for rg in wregs:
                dep(rg, False)
            if extra_waits:
                for e, v in extra_waits.items():
                    if waits.get(e, 0) < v:
                        waits[e] = v
        self.cnt[sem] = self.cnt.get(sem, 0) + inc
        cnt_after = self.cnt[sem]
        self.records.append((body, sem, fn, waits, inc, tag))
        for nm, p0, p1 in wregs:
            h = self.hist.setdefault(nm, [])
            h[:] = [r for r in h if r[2] <= p0 or r[1] >= p1]
            h.append([True, p0, p1, sem, cnt_after])
        for nm, p0, p1 in rregs:
            self.hist.setdefault(nm, []).append([False, p0, p1, sem, cnt_after])

    def dma(self, q, out_ap, in_ap, tag="", touch=None, key=None):
        sem = ("dk_" + key) if key else q
        extra = None
        if key is None and K.DMA_SERIAL and self.cnt.get(q, 0):
            extra = {q: self.cnt[q]}
        self.op(q, lambda e, o=out_ap, i=in_ap: e.dma_start(out=o, in_=i),
                reads=[in_ap], writes=[out_ap], tag=tag, extra_waits=extra,
                body=QUEUE_OWNER[q], sem=sem, inc=16)

    def dmat(self, q, out_ap, in_ap, tag="", key=None):
        sem = ("dk_" + key) if key else q
        extra = None
        if key is None and K.DMA_SERIAL and self.cnt.get(q, 0):
            extra = {q: self.cnt[q]}
        self.op(q, lambda e, o=out_ap, i=in_ap: e.dma_start_transpose(o, i),
                reads=[in_ap], writes=[out_ap], tag=tag, extra_waits=extra,
                body=QUEUE_OWNER[q], sem=sem, inc=16)

    def mm(self, out, lhsT, rhs, start, stop, tag=""):
        rd = [lhsT, rhs] + ([] if start else [out])
        self.op("tensor", lambda e, o=out, l=lhsT, r=rhs, s1=start, s2=stop:
                e.matmul(o, l, r, start=s1, stop=s2), reads=rd, writes=[out], tag=tag)

    def trans(self, out_ps, in_sb, idn, tag=""):
        self.op("tensor", lambda e, o=out_ps, i=in_sb, d=idn:
                e.transpose(o, i, d), reads=[in_sb, idn], writes=[out_ps], tag=tag)

    def act(self, out, in_, func, bias=0.0, scale=1.0, accum=None, tag=""):
        rd = [in_] + [x for x in (bias, scale) if isinstance(x, bass.AP)]
        wr = [out] + ([accum] if accum is not None else [])
        self.op("scalar", lambda e, o=out, i=in_, f=func, b=bias, s=scale, a=accum:
                (e.activation(o, i, f, bias=b, scale=s) if a is None else
                 e.activation(o, i, f, bias=b, scale=s, accum_out=a)),
                reads=rd, writes=wr, tag=tag)

    def tt(self, out, in0, in1, alu, eng="vector", tag=""):
        self.op(eng, lambda e, o=out, a=in0, b=in1, u=alu:
                e.tensor_tensor(o, a, b, u), reads=[in0, in1], writes=[out], tag=tag)

    def ts(self, out, in0, s1, s2, op0, op1=None, eng="vector", tag=""):
        rd = [in0] + [x for x in (s1, s2) if isinstance(x, bass.AP)]
        self.op(eng, lambda e, o=out, i=in0, a=s1, b=s2, u=op0, v=op1:
                (e.tensor_scalar(o, i, a, None, u) if v is None else
                 e.tensor_scalar(o, i, a, b, u, v)), reads=rd, writes=[out], tag=tag)

    def stt(self, out, in0, scal, in1, op0, op1, accum=None, eng="vector", tag=""):
        rd = [in0, in1] + ([scal] if isinstance(scal, bass.AP) else [])
        wr = [out] + ([accum] if accum is not None else [])
        self.op(eng, lambda e, o=out, i=in0, s=scal, j=in1, u=op0, v=op1, a=accum:
                e.scalar_tensor_tensor(o, i, s, j, u, v, accum_out=a),
                reads=rd, writes=wr, tag=tag)

    def tts(self, out, d0, d1, init, tag=""):
        rd = [d0, d1] + ([init] if isinstance(init, bass.AP) else [])
        self.op("vector", lambda e, o=out, a=d0, b=d1, i=init:
                e.tensor_tensor_scan(o, a, b, i, ALU.mult, ALU.add),
                reads=rd, writes=[out], tag=tag)

    def red(self, out, in_, alu, absval=False, negate=False, tag=""):
        self.op("vector", lambda e, o=out, i=in_, u=alu, av=absval, ng=negate:
                e.tensor_reduce(o, i, AX.X, u, apply_absolute_value=av, negate=ng),
                reads=[in_], writes=[out], tag=tag)

    def recip(self, out, in_, tag=""):
        self.op("vector", lambda e, o=out, i=in_: e.reciprocal(o, i),
                reads=[in_], writes=[out], tag=tag)

    def copy(self, out, in_, eng="vector", tag=""):
        self.op(eng, lambda e, o=out, i=in_: e.tensor_copy(o, i),
                reads=[in_], writes=[out], tag=tag)

    def memset(self, ap, val, eng="vector", tag=""):
        self.op(eng, lambda e, o=ap, v=val: e.memset(o, v),
                reads=[], writes=[ap], tag=tag)

    def finalize(self):
        if K.GLOBAL_SEQ:
            return self._finalize_global()
        nc = self.nc
        sems = {}
        for e, v in self.cnt.items():
            if v > 0:
                sems[e] = self.es.enter_context(nc.semaphore("sem_" + e))

        def body(body_name):
            def emit(eng):
                wd = {}
                for rec_body, rec_sem, fn, waits, inc, tag in self.records:
                    if rec_body != body_name:
                        continue
                    for we in sorted(waits):
                        wv = waits[we]
                        if we not in sems:
                            continue
                        if we == rec_sem and rec_sem == "tensor":
                            continue
                        if wd.get(we, 0) >= wv:
                            continue
                        eng.wait_ge(sems[we], wv)
                        wd[we] = wv
                    fn(eng).then_inc(sems[rec_sem], inc)
                if body_name == "sync":
                    for e in sems:
                        eng.wait_ge(sems[e], self.cnt[e])
            return emit

        block = self.es.enter_context(nc.Block())
        block.sync(body("sync"))
        block.vector(body("vector"))
        block.scalar(body("scalar"))
        block.tensor(body("tensor"))
        block.gpsimd(body("gpsimd"))
        return nc

    def _finalize_global(self):
        nc = self.nc
        seq = self.es.enter_context(nc.semaphore("seq"))
        block = self.es.enter_context(nc.Block())
        sched = []
        cum = 0
        for body_e, sem, fn, waits, inc, tag in self.records:
            sched.append((body_e, fn, cum, inc))
            cum += inc
        total = cum

        def body(body_name):
            def emit(eng):
                for rec_body, fn, before, inc in sched:
                    if rec_body != body_name:
                        continue
                    if before > 0:
                        eng.wait_ge(seq, before)
                    fn(eng).then_inc(seq, inc)
                if body_name == "sync":
                    eng.wait_ge(seq, total)
            return emit

        block.sync(body("sync"))
        block.vector(body("vector"))
        block.scalar(body("scalar"))
        block.tensor(body("tensor"))
        block.gpsimd(body("gpsimd"))
        return nc

    def run(self, in_maps):
        nc = self.finalize()
        res = run_bass_kernel_spmd(nc, in_maps, core_ids=list(range(len(in_maps))))
        self.es.close()
        return res



def rope_tables():
    inv = 1.0 / 10000.0 ** (np.arange(0, 64, 2, dtype=np.float32) / 64.0)
    pos = np.arange(T, dtype=np.float32)
    fr = np.outer(inv, pos)
    cos = np.cos(fr).astype(np.float32)
    sin = np.sin(fr).astype(np.float32)
    cos64 = np.concatenate([cos, cos], 0)
    ss64 = np.concatenate([-sin, sin], 0)
    return (np.ascontiguousarray(np.concatenate([cos64, cos64], 0)),
            np.ascontiguousarray(np.concatenate([ss64, ss64], 0)))


def causal_mask128():
    m = np.zeros((128, 128), np.float32)
    iu = np.triu_indices(128, 1)
    m[iu] = -1e9
    return m


IDN128 = np.eye(128, dtype=np.float32).astype(BF16)
ONES128 = np.ones((128, 1), dtype=np.float32)



def emit_cast_stream_bf16(k, src_f32, dst_b, nchunk=16, width=T):
    with k.scope():
        stg = [k.sb(f"csf{i}", [128, width], F32) for i in range(2)]
        stb = [k.sb(f"csb{i}", [128, width], BF) for i in range(2)]
        for c in range(nchunk):
            sf, sb_ = stg[c % 2], stb[c % 2]
            k.dma("dsync", sf[:], src_f32[c * 128:(c + 1) * 128, :], key=f"cs{c % 2}")
            k.act(sb_[:], sf[:], AF.Copy, tag=f"cast{c}")
            k.dma("dgps", dst_b[c * 128:(c + 1) * 128, :], sb_[:], key=f"cb{c % 2}")


def emit_ln_fm(k, srow, sqrow, upre, xout, xout_b, rowscr, eps):
    mrow = k.sb("mrow", [1, T], F32)
    vrow = k.sb("vrow", [1, T], F32)
    rrow = k.sb("rrow", [1, T], F32)
    k.ts(mrow[0:1, :], srow[0:1, :], 1.0 / D, None, ALU.mult, tag="mean")
    k.act(vrow[0:1, :], mrow[0:1, :], AF.Square, tag="m2")
    k.stt(vrow[0:1, :], sqrow[0:1, :], 1.0 / D, vrow[0:1, :], ALU.mult,
          ALU.subtract, tag="var")
    ecol = k.sb("ecol", [1, 1], F32)
    k.memset(ecol[0:1, 0:1], eps)
    k.act(rrow[0:1, :], vrow[0:1, :], AF.Ln, bias=ecol[0:1, 0:1], tag="lnv")
    k.act(rrow[0:1, :], rrow[0:1, :], AF.Exp, scale=-0.5, tag="rstd")
    k.dma("dsync", rowscr[0:1, :], mrow[0:1, :])
    k.dma("dsync", rowscr[1:2, :], rrow[0:1, :])
    mb = k.sb("mb", [128, T], F32)
    rb = k.sb("rb", [128, T], F32)
    k.dma("dsync", mb[:], rowscr[0:1, :].partition_broadcast(128))
    k.dma("dsync", rb[:], rowscr[1:2, :].partition_broadcast(128))
    ut = [k.sb(f"lnu{i}", [128, T], F32) for i in range(2)]
    xo = [k.sb(f"lnx{i}", [128, T], F32) for i in range(2)]
    xb = [k.sb(f"lnb{i}", [128, T], BF) for i in range(2)]
    for m in range(16):
        u, x, b = ut[m % 2], xo[m % 2], xb[m % 2]
        k.dma("dsync", u[:], upre[m * 128:(m + 1) * 128, :], key=f"lnu{m % 2}")
        k.tt(x[:], u[:], mb[:], ALU.subtract, tag="ln_sub")
        k.tt(x[:], x[:], rb[:], ALU.mult, tag="ln_mul")
        k.dma("dgps", xout[m * 128:(m + 1) * 128, :], x[:], tag="ln_out", key=f"lnx{m % 2}")
        if xout_b is not None:
            k.act(b[:], x[:], AF.Copy, tag="ln_cast")
            k.dma("dgps", xout_b[m * 128:(m + 1) * 128, :], b[:], tag="ln_outb", key=f"lnb{m % 2}")


def rope_fm(k, out_b, qf, qscr, cos_s, ss_s, qswap, tmp, npart):
    k.dma("dgps", qscr[0:npart, :], qf[0:npart, :])
    for h0 in range(0, npart, 64):
        k.dma("dsync", qswap[h0:h0 + 32, :], qscr[h0 + 32:h0 + 64, :])
        k.dma("dsync", qswap[h0 + 32:h0 + 64, :], qscr[h0:h0 + 32, :])
    k.tt(tmp[0:npart, :], qf[0:npart, :], cos_s[0:npart, :], ALU.mult, tag="rp_c")
    k.tt(qswap[0:npart, :], qswap[0:npart, :], ss_s[0:npart, :], ALU.mult, tag="rp_s")
    k.tt(out_b[0:npart, :], tmp[0:npart, :], qswap[0:npart, :], ALU.add, tag="rp_a")



def build_phase_A(debug=False, nheads=16, do_wo=True):
    k = K("l0attn")
    xT = k.din("xT", [D, T], F32)
    wqd = k.din("wqd", [2048, 768], F32)
    wqu = k.din("wqu", [768, 2048], F32)
    wqr = k.din("wqr", [768, 1024], F32)
    wkv = k.din("wkv", [2048, 512], F32)
    wku = k.din("wku", [512, 2048], F32)
    wvu = k.din("wvu", [512, 2048], F32)
    wkr = k.din("wkr", [2048, 64], F32)
    wo = k.din("wo", [2048, 2048], F32)
    cosc = k.din("cosc", [128, T], F32)
    ssc = k.din("ssc", [128, T], F32)
    idn = k.din("idn", [128, 128], BF)
    ones = k.din("ones", [128, 1], F32)
    cmask = k.din("cmask", [128, 128], F32)

    xs1 = k.dout("xs1", [D, T], F32)
    xs1b = k.dout("xs1b", [D, T], BF)
    dd = k.dout if debug else k.dtmp
    xbf = dd("xbf", [D, T], BF)
    afm = dd("afm", [2048, T], BF)
    upre = dd("upre", [D, T], F32)
    rowscr = k.dtmp("rowscr", [4, T], F32)
    krscr = k.dtmp("krscr", [64, T], BF)
    qscr = k.dtmp("qscr", [128, T], F32)

    SC = 1.0 / math.sqrt(192.0)

    cos_s = k.sb("cos_s", [128, T], F32)
    ss_s = k.sb("ss_s", [128, T], F32)
    idn_s = k.sb("idn_s", [128, 128], BF)
    ones_s = k.sb("ones_s", [128, 1], F32)
    cm_s = k.sb("cm_s", [128, 128], F32)
    k.dma("dsync", cos_s[:], cosc[:])
    k.dma("dsync", ss_s[:], ssc[:])
    k.dma("dsync", idn_s[:], idn[:])
    k.dma("dsync", ones_s[:], ones[:])
    k.dma("dsync", cm_s[:], cmask[:])

    psA = k.ps("psA", [128, 512])
    psB = k.ps("psB", [128, 512])
    psS = k.ps("psS", [128, 2048])
    psT = k.ps("psT", [128, 128], BF)

    emit_cast_stream_bf16(k, xT, xbf)

    with k.scope():
        qlat_b = [k.sb(f"qlat{c}", [128, T], BF) for c in range(6)]
        kvl_b = [k.sb(f"kvl{c}", [128, T], BF) for c in range(4)]
        krb = k.sb("krb", [128, T], BF)
        krf0 = k.sb("krf0", [64, T], F32)
        qr_b = [k.sb(f"qr{c}", [128, T], BF) for c in range(8)]
        wqu_t = [k.sb(f"wqu{c}", [128, 2048], BF) for c in range(6)]
        wku_t = [k.sb(f"wku{c}", [128, 2048], BF) for c in range(4)]
        wvu_t = [k.sb(f"wvu{c}", [128, 2048], BF) for c in range(4)]

        with k.scope():
            stg = k.sb("stgw", [128, 768], F32)
            xsb = [k.sb(f"xsb{i}", [128, 512], BF) for i in range(4)]
            wqd_t = [k.sb(f"wqd{c}", [128, 768], BF) for c in range(16)]
            for c in range(16):
                k.dma("dsync", stg[:], wqd[c * 128:(c + 1) * 128, :])
                k.act(wqd_t[c][:], stg[:], AF.Copy)
            ib = 0
            for m in range(6):
                pp = psA if m % 2 == 0 else psB
                for n in range(4):
                    for kk in range(16):
                        s = xsb[ib % 4]; ib += 1
                        k.dma("dgps", s[:], xbf[kk * 128:(kk + 1) * 128,
                                                n * 512:(n + 1) * 512], key=f"x{ib % 4}")
                        k.mm(pp[:], wqd_t[kk][:, m * 128:(m + 1) * 128], s[:],
                             start=(kk == 0), stop=(kk == 15), tag=f"qlat{m}")
                    k.act(qlat_b[m][:, n * 512:(n + 1) * 512], pp[:], AF.Copy)

        with k.scope():
            stg = k.sb("stgw2", [128, 512], F32)
            xsb = [k.sb(f"xsc{i}", [128, 512], BF) for i in range(4)]
            wkv_t = [k.sb(f"wkv{c}", [128, 512], BF) for c in range(16)]
            wkr_t = [k.sb(f"wkr{c}", [128, 64], BF) for c in range(16)]
            for c in range(16):
                k.dma("dsync", stg[:], wkv[c * 128:(c + 1) * 128, :])
                k.act(wkv_t[c][:], stg[:], AF.Copy)
                k.dma("dsync", stg[:, 0:64], wkr[c * 128:(c + 1) * 128, :])
                k.act(wkr_t[c][:], stg[:, 0:64], AF.Copy)
            ib = 0
            for m in range(4):
                pp = psA if m % 2 == 0 else psB
                for n in range(4):
                    for kk in range(16):
                        s = xsb[ib % 4]; ib += 1
                        k.dma("dgps", s[:], xbf[kk * 128:(kk + 1) * 128,
                                                n * 512:(n + 1) * 512], key=f"x{ib % 4}")
                        k.mm(pp[:], wkv_t[kk][:, m * 128:(m + 1) * 128], s[:],
                             start=(kk == 0), stop=(kk == 15), tag=f"kvl{m}")
                    k.act(kvl_b[m][:, n * 512:(n + 1) * 512], pp[:], AF.Copy)
            for n in range(4):
                for kk in range(16):
                    s = xsb[ib % 4]; ib += 1
                    k.dma("dgps", s[:], xbf[kk * 128:(kk + 1) * 128,
                                            n * 512:(n + 1) * 512])
                    k.mm(psA[0:64, :], wkr_t[kk][:, 0:64], s[:],
                         start=(kk == 0), stop=(kk == 15), tag="krope")
                k.act(krf0[0:64, n * 512:(n + 1) * 512], psA[0:64, :], AF.Copy)

        with k.scope():
            stg = k.sb("stgh", [128, 2048], F32)
            for c in range(6):
                k.dma("dsync", stg[:], wqu[c * 128:(c + 1) * 128, :])
                k.act(wqu_t[c][:], stg[:], AF.Copy)
            for c in range(4):
                k.dma("dsync", stg[:], wku[c * 128:(c + 1) * 128, :])
                k.act(wku_t[c][:], stg[:], AF.Copy)
            for c in range(4):
                k.dma("dsync", stg[:], wvu[c * 128:(c + 1) * 128, :])
                k.act(wvu_t[c][:], stg[:], AF.Copy)

        with k.scope():
            stg = k.sb("stgq", [128, 1024], F32)
            wqr_t = [k.sb(f"wqr{c}", [128, 1024], BF) for c in range(6)]
            qrf = k.sb("qrf", [128, T], F32)
            rtmp = k.sb("rtmp", [128, T], F32)
            rtmp2 = k.sb("rtmp2", [128, T], F32)
            for c in range(6):
                k.dma("dsync", stg[:], wqr[c * 128:(c + 1) * 128, :])
                k.act(wqr_t[c][:], stg[:], AF.Copy)
            rope_fm(k, krb, krf0, qscr, cos_s, ss_s, rtmp2, rtmp, 64)
            k.dma("dgps", krscr[:], krb[0:64, :])
            k.dma("dgps", krb[64:128, :], krscr[:])
            for m in range(8):
                pp = psA if m % 2 == 0 else psB
                for n in range(4):
                    for kk in range(6):
                        k.mm(pp[:], wqr_t[kk][:, m * 128:(m + 1) * 128],
                             qlat_b[kk][:, n * 512:(n + 1) * 512],
                             start=(kk == 0), stop=(kk == 5), tag=f"qrope{m}")
                    k.act(qrf[:, n * 512:(n + 1) * 512], pp[:], AF.Copy)
                rope_fm(k, qr_b[m], qrf, qscr, cos_s, ss_s, rtmp2, rtmp, 128)

        if debug:
            dbg_q = k.dout("dbg_qlat", [768, T], BF)
            for c in range(6):
                k.dma("dgps", dbg_q[c * 128:(c + 1) * 128, :], qlat_b[c][:])
            dbg_kv = k.dout("dbg_kvl", [512, T], BF)
            for c in range(4):
                k.dma("dgps", dbg_kv[c * 128:(c + 1) * 128, :], kvl_b[c][:])
            dbg_kr = k.dout("dbg_krb", [128, T], BF)
            k.dma("dgps", dbg_kr[:], krb[:])
            dbg_qr = k.dout("dbg_qr", [1024, T], BF)
            for c in range(8):
                k.dma("dgps", dbg_qr[c * 128:(c + 1) * 128, :], qr_b[c][:])

        with k.scope():
            qc_h = k.sb("qc_h", [128, T], BF)
            kc_h = k.sb("kc_h", [128, T], BF)
            vt_h = k.sb("vt_h", [128, T], BF)
            probs = k.sb("probs", [128, T], BF)
            pT = k.sb("pT", [128, T], BF)
            att_s = k.sb("att_s", [128, 128], BF)
            nm = k.sb("nm", [128, 1], F32)
            nms = k.sb("nms", [128, 1], F32)
            rs = k.sb("rs", [128, 1], F32)
            rinv = k.sb("rinv", [128, 1], F32)
            for h in range(nheads):
                for n in range(4):
                    for kk in range(6):
                        k.mm(psA[:], wqu_t[kk][:, h * 128:(h + 1) * 128],
                             qlat_b[kk][:, n * 512:(n + 1) * 512],
                             start=(kk == 0), stop=(kk == 5), tag=f"qc{h}")
                    k.act(qc_h[:, n * 512:(n + 1) * 512], psA[:], AF.Copy)
                for n in range(4):
                    for kk in range(4):
                        k.mm(psB[:], wku_t[kk][:, h * 128:(h + 1) * 128],
                             kvl_b[kk][:, n * 512:(n + 1) * 512],
                             start=(kk == 0), stop=(kk == 3), tag=f"kc{h}")
                    k.act(kc_h[:, n * 512:(n + 1) * 512], psB[:], AF.Copy)
                for tb in range(16):
                    for kk in range(4):
                        k.mm(psA[:, 0:128], kvl_b[kk][:, tb * 128:(tb + 1) * 128],
                             wvu_t[kk][:, h * 128:(h + 1) * 128],
                             start=(kk == 0), stop=(kk == 3), tag=f"v{h}")
                    k.act(vt_h[:, tb * 128:(tb + 1) * 128], psA[:, 0:128], AF.Copy)
                qrh = qr_b[h // 2][(h % 2) * 64:(h % 2) * 64 + 64, :]
                if debug and h == 0:
                    dqc = k.dout("dbg_qc0", [128, T], BF)
                    k.dma("dgps", dqc[:], qc_h[:])
                    dkc = k.dout("dbg_kc0", [128, T], BF)
                    k.dma("dgps", dkc[:], kc_h[:])
                    dvt = k.dout("dbg_vt0", [128, T], BF)
                    k.dma("dgps", dvt[:], vt_h[:])
                for qt in range(16):
                    W = (qt + 1) * 128
                    for c0 in range(0, W, 512):
                        cw = min(512, W - c0)
                        k.mm(psS[:, c0:c0 + cw], qc_h[:, qt * 128:(qt + 1) * 128],
                             kc_h[:, c0:c0 + cw], start=True, stop=False,
                             tag=f"sc{h}.{qt}")
                        hb = (h % 2) * 64
                        k.mm(psS[:, c0:c0 + cw], qrh[:, qt * 128:(qt + 1) * 128],
                             krb[hb:hb + 64, c0:c0 + cw], start=False, stop=True,
                             tag=f"sr{h}.{qt}")
                    if debug and h == 0 and qt == 0:
                        dpm = k.dout("dbg_premask", [128, 128], F32)
                        pmt = k.sb("pmt", [128, 128], F32)
                        k.copy(pmt[:], psS[:, 0:128], tag="dbgpm")
                        k.dma("dgps", dpm[:], pmt[:])
                    k.tt(psS[:, qt * 128:W], psS[:, qt * 128:W], cm_s[:],
                         ALU.add, tag="cmask")
                    k.red(nm[:, 0:1], psS[:, 0:W], ALU.max, negate=True, tag="nmax")
                    k.ts(nms[:, 0:1], nm[:, 0:1], SC, None, ALU.mult)
                    k.act(probs[:, 0:W], psS[:, 0:W], AF.Exp, bias=nms[:, 0:1],
                          scale=SC, accum=rs[:, 0:1], tag=f"sm{h}.{qt}")
                    k.recip(rinv[:, 0:1], rs[:, 0:1])
                    if debug and h == 0 and qt in (0, 15):
                        dsc = k.dout(f"dbg_sc{qt}", [128, T], F32)
                        k.copy(k.sb(f"dbsc{qt}", [128, T], F32)[:, 0:W], psS[:, 0:W], tag="dbgsc") if False else None
                        stile = k.sb(f"dbs{qt}", [128, T], F32)
                        k.copy(stile[:, 0:W], psS[:, 0:W], tag="dbgcp")
                        k.dma("dgps", dsc[:, 0:W], stile[:, 0:W])
                        dpr = k.dout(f"dbg_pr{qt}", [128, T], BF)
                        k.dma("dgps", dpr[:, 0:W], probs[:, 0:W])
                        drs = k.dout(f"dbg_rs{qt}", [128, 1], F32)
                        k.dma("dgps", drs[:], rs[:])
                    for kt in range(qt + 1):
                        k.trans(psT[:], probs[:, kt * 128:(kt + 1) * 128], idn_s[:],
                                tag="pt")
                        k.copy(pT[:, kt * 128:(kt + 1) * 128], psT[:], tag="ptev")
                    for kt in range(qt + 1):
                        k.mm(psB[:, 0:128], pT[:, kt * 128:(kt + 1) * 128],
                             vt_h[:, kt * 128:(kt + 1) * 128],
                             start=(kt == 0), stop=(kt == qt), tag=f"av{h}.{qt}")
                    k.act(att_s[:], psB[:, 0:128], AF.Copy, scale=rinv[:, 0:1],
                          tag="avev")
                    k.trans(psT[:], att_s[:], idn_s[:], tag="att_t")
                    k.copy(att_s[:], psT[:], tag="att_tev")
                    k.dma("dgps", afm[h * 128:(h + 1) * 128,
                                      qt * 128:(qt + 1) * 128], att_s[:], tag="ast", key="ast")

    if not do_wo:
        return k
    with k.scope():
        wo_t = [k.sb(f"wo{c}", [128, 2048], BF) for c in range(16)]
        stg = k.sb("stgo", [128, 2048], F32)
        for c in range(16):
            k.dma("dsync", stg[:], wo[c * 128:(c + 1) * 128, :])
            k.act(wo_t[c][:], stg[:], AF.Copy)
        af_st = [k.sb(f"afst{i}", [128, 512], BF) for i in range(4)]
        u_t = k.sb("u_t", [128, 512], F32)
        usq = k.sb("usq", [128, 512], F32)
        xres = k.sb("xres", [128, 512], F32)
        srow = k.sb("srow", [1, T], F32)
        sqrow = k.sb("sqrow", [1, T], F32)
        ib = 0
        for n in range(4):
            for m in range(16):
                for kk in range(16):
                    s = af_st[ib % 4]; ib += 1
                    k.dma("dsync", s[:], afm[kk * 128:(kk + 1) * 128,
                                             n * 512:(n + 1) * 512], key=f"af{ib % 4}")
                    k.mm(psA[:], wo_t[kk][:, m * 128:(m + 1) * 128], s[:],
                         start=(kk == 0), stop=(kk == 15), tag=f"wo{m}.{n}")
                k.dma("dgps", xres[:], xT[m * 128:(m + 1) * 128,
                                          n * 512:(n + 1) * 512], key="xres")
                k.tt(u_t[:], xres[:], psA[:], ALU.add, tag="resid")
                k.dma("dgps", upre[m * 128:(m + 1) * 128,
                                   n * 512:(n + 1) * 512], u_t[:], tag="ust", key="ust")
                k.act(usq[:], u_t[:], AF.Square, tag="usq")
                k.mm(psS[0:1, 0:512], ones_s[:], u_t[:], start=(m == 0),
                     stop=(m == 15), tag="stS")
                k.mm(psS[0:1, 512:1024], ones_s[:], usq[:], start=(m == 0),
                     stop=(m == 15), tag="stQ")
            k.copy(srow[0:1, n * 512:(n + 1) * 512], psS[0:1, 0:512], tag="sr")
            k.copy(sqrow[0:1, n * 512:(n + 1) * 512], psS[0:1, 512:1024], tag="sq")
        emit_ln_fm(k, srow, sqrow, upre, xs1, xs1b, rowscr, eps=1e-5)
    return k




def emit_quant(k, wsrc, nrows, ncols, gper, wqdst, tag):
    with k.scope():
        wb = [k.sb(f"qw{tag}{i}", [128, ncols], BF) for i in range(2)]
        sa = k.sb(f"qsa{tag}", [128, gper], F32)
        scb = k.sb(f"qscb{tag}", [128, gper], BF)
        scf = k.sb(f"qscf{tag}", [128, gper], F32)
        rec = k.sb(f"qrec{tag}", [128, gper], F32)
        rb = [k.sb(f"qrb{tag}{i}", [128, ncols], BF) for i in range(2)]
        tb = [k.sb(f"qtb{tag}{i}", [128, ncols], BF) for i in range(2)]
        qf = [k.sb(f"qqf{tag}{i}", [128, ncols], F32) for i in range(2)]
        for c in range(nrows // 128):
            w, r, t, q = wb[c % 2], rb[c % 2], tb[c % 2], qf[c % 2]
            k.dma("dgps", w[:], wsrc[c * 128:(c + 1) * 128, :], tag=f"qld{tag}", key=f"qw{c % 2}")
            k.red(sa[:], w[:].rearrange("p (g f) -> p g f", f=64), ALU.add,
                  absval=True, tag="qred")
            k.ts(sa[:], sa[:], 1.0 / 64.0, 1e-8, ALU.mult, ALU.max, tag="qsc")
            k.copy(scb[:], sa[:], tag="qscb")
            k.copy(scf[:], scb[:], tag="qscf")
            k.recip(rec[:], scf[:], tag="qrec")
            k.tt(r[:].rearrange("p (g f) -> p g f", f=64),
                 w[:].rearrange("p (g f) -> p g f", f=64),
                 rec[:].unsqueeze(2).broadcast_to((128, gper, 64)),
                 ALU.mult, tag="qr")
            k.ts(t[:], r[:], 192.0, None, ALU.add, tag="qt")
            k.ts(q[:], t[:], -192.0, 1.0, ALU.add, ALU.min, tag="qq")
            k.ts(q[:], q[:], -1.0, None, ALU.max, tag="qq2")
            k.tt(q[:].rearrange("p (g f) -> p g f", f=64),
                 q[:].rearrange("p (g f) -> p g f", f=64),
                 scf[:].unsqueeze(2).broadcast_to((128, gper, 64)),
                 ALU.mult, tag="qm")
            k.copy(w[:], q[:], tag="qcast")
            k.dma("dsync", wqdst[c * 128:(c + 1) * 128, :], w[:], tag=f"qst{tag}", key=f"qs{c % 2}")


def build_phase_B(debug=False):
    k = K("l0ffn")
    xs1 = k.din("xs1", [D, T], F32)
    xs1b = k.din("xs1b", [D, T], BF)
    wgu = k.din("wgu", [16384, 2048], F32)
    wpr = k.din("wpr", [2048, 8192], F32)
    ones = k.din("ones", [128, 1], F32)
    xs2 = k.dout("xs2", [D, T], F32)
    xs2b = k.dout("xs2b", [D, T], BF)
    dd = k.dout if debug else k.dtmp
    wguq = dd("wguq", [16384, 2048], BF)
    wprq = dd("wprq", [2048, 8192], BF)
    upre = k.dtmp("upreB", [D, T], F32)
    rowscr = k.dtmp("rowscrB", [4, T], F32)
    hrow_scr = k.dtmp("hrowB", [2, 512], F32)

    ones_s = k.sb("ones_s", [128, 1], F32)
    k.dma("dsync", ones_s[:], ones[:])

    emit_quant(k, wgu, 16384, 2048, 32, wguq, "g")
    emit_quant(k, wpr, 2048, 8192, 128, wprq, "p")

    TC = 512
    for tc in range(4):
        t0 = tc * TC
        with k.scope():
            hdn = [k.sb(f"hdn{i}", [128, TC], BF) for i in range(64)]
            srow = k.sb("srowB", [1, TC], F32)
            rstdb = k.sb("rstdbB", [128, TC], F32)
            with k.scope():
                psG = k.ps(f"psG{tc}", [128, 2048])
                psU = k.ps(f"psU{tc}", [128, 1536])
                psSt = k.ps(f"psSt{tc}", [1, 512])
                wgT = [k.sb(f"wgT{i}", [128, 384], BF) for i in range(4)]
                wuT = [k.sb(f"wuT{i}", [128, 384], BF) for i in range(4)]
                xst = [k.sb(f"xst{i}", [128, TC], BF) for i in range(4)]
                sg = k.sb("sgB", [128, TC], BF)
                sq = k.sb("sqB", [128, TC], F32)
                first_stat = [True]
                for ig in range(22):
                    iw = min(3, 64 - ig * 3)
                    if iw <= 0:
                        break
                    for kk in range(16):
                        g_, u_, x_ = wgT[kk % 4], wuT[kk % 4], xst[kk % 4]
                        k.dmat("dsync", g_[:, 0:iw * 128],
                               wguq[ig * 384:ig * 384 + iw * 128,
                                    kk * 128:(kk + 1) * 128], tag="wgT", key=f"wg{kk % 4}")
                        k.dmat("dsync", u_[:, 0:iw * 128],
                               wguq[8192 + ig * 384:8192 + ig * 384 + iw * 128,
                                    kk * 128:(kk + 1) * 128], tag="wuT", key=f"wu{kk % 4}")
                        k.dma("dgps", x_[:], xs1b[kk * 128:(kk + 1) * 128,
                                                  t0:t0 + TC], tag="xld", key=f"x{kk % 4}")
                        for j in range(iw):
                            k.mm(psG[:, j * TC:(j + 1) * TC],
                                 g_[:, j * 128:(j + 1) * 128], x_[:],
                                 start=(kk == 0), stop=(kk == 15), tag="gmm")
                            k.mm(psU[:, j * TC:(j + 1) * TC],
                                 u_[:, j * 128:(j + 1) * 128], x_[:],
                                 start=(kk == 0), stop=(kk == 15), tag="umm")
                    for j in range(iw):
                        i = ig * 3 + j
                        k.act(sg[:], psG[:, j * TC:(j + 1) * TC], AF.Silu, tag="silu")
                        k.stt(hdn[i][:], psU[:, j * TC:(j + 1) * TC], 1.0, sg[:],
                              ALU.mult, ALU.mult, tag="hmul")
                        k.act(sq[:], hdn[i][:], AF.Square, tag="hsq")
                        k.mm(psSt[0:1, :], ones_s[:], sq[:],
                             start=first_stat[0], stop=(i == 63), tag="hst")
                        first_stat[0] = False
                k.copy(srow[0:1, :], psSt[0:1, :], tag="hsrow")
                ecol = k.sb("ecolB", [1, 1], F32)
                k.memset(ecol[0:1, 0:1], float(np.finfo(np.float32).eps))
                k.ts(srow[0:1, :], srow[0:1, :], 1.0 / 8192.0, None, ALU.mult,
                     tag="hms")
                k.act(srow[0:1, :], srow[0:1, :], AF.Ln, bias=ecol[0:1, 0:1],
                      tag="hln")
                k.act(srow[0:1, :], srow[0:1, :], AF.Exp, scale=-0.5, tag="hexp")
                k.dma("dsync", hrow_scr[0:1, :], srow[0:1, :])
                k.dma("dsync", rstdb[:], hrow_scr[0:1, :].partition_broadcast(128))
            if debug and tc == 0:
                dh = k.dout("dbg_hdn", [8192, 512], BF)
                for i in range(64):
                    k.dma("dgps", dh[i * 128:(i + 1) * 128, :], hdn[i][:])
                dsr = k.dout("dbg_rstd", [128, 512], F32)
                k.dma("dgps", dsr[:], rstdb[:])
            for i in range(64):
                k.tt(hdn[i][:], hdn[i][:], rstdb[:], ALU.mult, tag="hnorm")
            with k.scope():
                psD = k.ps(f"psD{tc}", [128, 2048])
                psL = k.ps(f"psL{tc}", [1, 1024])
                wpT = [k.sb(f"wpT{i}", [128, 512], BF) for i in range(4)]
                u_t = k.sb("u_tB", [128, TC], F32)
                usq = k.sb("usqB", [128, TC], F32)
                xres = k.sb("xresB", [128, TC], F32)
                lsrow = k.sb("lsrowB", [1, TC], F32)
                lqrow = k.sb("lqrowB", [1, TC], F32)
                for mg in range(4):
                    for kk in range(64):
                        w_ = wpT[kk % 4]
                        k.dmat("dsync", w_[:],
                               wprq[mg * 512:(mg + 1) * 512,
                                    kk * 128:(kk + 1) * 128], tag="wpT", key=f"wp{kk % 4}")
                        for j in range(4):
                            k.mm(psD[:, j * TC:(j + 1) * TC],
                                 w_[:, j * 128:(j + 1) * 128], hdn[kk][:],
                                 start=(kk == 0), stop=(kk == 63), tag="dmm")
                    for j in range(4):
                        m = mg * 4 + j
                        k.dma("dgps", xres[:], xs1[m * 128:(m + 1) * 128,
                                                   t0:t0 + TC], tag="xr", key="xres")
                        k.tt(u_t[:], xres[:], psD[:, j * TC:(j + 1) * TC],
                             ALU.add, tag="resB")
                        k.dma("dgps", upre[m * 128:(m + 1) * 128, t0:t0 + TC],
                              u_t[:], tag="uStB", key="ust")
                        k.act(usq[:], u_t[:], AF.Square, tag="usqB")
                        k.mm(psL[0:1, 0:512], ones_s[:], u_t[:],
                             start=(m == 0), stop=(m == 15), tag="stSB")
                        k.mm(psL[0:1, 512:1024], ones_s[:], usq[:],
                             start=(m == 0), stop=(m == 15), tag="stQB")
                k.copy(lsrow[0:1, :], psL[0:1, 0:512], tag="lsr")
                k.copy(lqrow[0:1, :], psL[0:1, 512:1024], tag="lqr")
                emit_ln_cols(k, lsrow, lqrow, upre, xs2, xs2b, rowscr,
                             1e-5, t0, TC, tc)
    return k


def emit_ln_cols(k, srow, sqrow, upre, xout, xout_b, rowscr, eps, t0, TC, uid):
    mrow = k.sb(f"mrow{uid}", [1, TC], F32)
    vrow = k.sb(f"vrow{uid}", [1, TC], F32)
    rrow = k.sb(f"rrow{uid}", [1, TC], F32)
    k.ts(mrow[0:1, :], srow[0:1, :], 1.0 / D, None, ALU.mult, tag="meanC")
    k.act(vrow[0:1, :], mrow[0:1, :], AF.Square, tag="m2C")
    k.stt(vrow[0:1, :], sqrow[0:1, :], 1.0 / D, vrow[0:1, :], ALU.mult,
          ALU.subtract, tag="varC")
    ecol = k.sb(f"ecolC{uid}", [1, 1], F32)
    k.memset(ecol[0:1, 0:1], eps)
    k.act(rrow[0:1, :], vrow[0:1, :], AF.Ln, bias=ecol[0:1, 0:1], tag="lnvC")
    k.act(rrow[0:1, :], rrow[0:1, :], AF.Exp, scale=-0.5, tag="rstdC")
    k.dma("dsync", rowscr[0:1, t0:t0 + TC], mrow[0:1, :])
    k.dma("dsync", rowscr[1:2, t0:t0 + TC], rrow[0:1, :])
    mb = k.sb(f"mbC{uid}", [128, TC], F32)
    rb = k.sb(f"rbC{uid}", [128, TC], F32)
    k.dma("dsync", mb[:], rowscr[0:1, t0:t0 + TC].partition_broadcast(128))
    k.dma("dsync", rb[:], rowscr[1:2, t0:t0 + TC].partition_broadcast(128))
    ut = [k.sb(f"lnuC{uid}{i}", [128, TC], F32) for i in range(2)]
    xo = [k.sb(f"lnxC{uid}{i}", [128, TC], F32) for i in range(2)]
    xb = [k.sb(f"lnbC{uid}{i}", [128, TC], BF) for i in range(2)]
    for m in range(16):
        u, x, b = ut[m % 2], xo[m % 2], xb[m % 2]
        k.dma("dsync", u[:], upre[m * 128:(m + 1) * 128, t0:t0 + TC], key=f"lnu{m % 2}")
        k.tt(x[:], u[:], mb[:], ALU.subtract, tag="lnsubC")
        k.tt(x[:], x[:], rb[:], ALU.mult, tag="lnmulC")
        k.dma("dgps", xout[m * 128:(m + 1) * 128, t0:t0 + TC], x[:], tag="lnoC", key=f"lnx{m % 2}")
        if xout_b is not None:
            k.act(b[:], x[:], AF.Copy, tag="lncC")
            k.dma("dgps", xout_b[m * 128:(m + 1) * 128, t0:t0 + TC], b[:],
                  tag="lnobC", key=f"lnb{m % 2}")





def build_phase_C(debug=False):
    k = K("l1pre")
    xs2 = k.din("xs2", [D, T], F32)
    xs2b = k.din("xs2b", [D, T], BF)
    nw = k.din("nw", [2048, 1], F32)
    wselT = k.din("wselT", [2048, 2048], F32)
    selb = k.din("selb", [2048, 1], F32)
    selg = k.din("selg", [2048, 1], F32)
    wxpT = k.din("wxpT", [2048, 384], F32)
    wdtT = k.din("wdtT", [128, 2048], F32)
    dtb = k.din("dtb", [2048, 1], F32)
    ones = k.din("ones", [128, 1], F32)

    dtT = k.dout("dtT", [D, T], BF)
    sfm = k.dout("sfm", [D, T], BF)
    Brows = k.dout("Brows", [128, T], BF)
    Crows = k.dout("Crows", [128, T], BF)
    rowscr = k.dtmp("rowscrC", [4, T], F32)

    ones_s = k.sb("ones_sC", [128, 1], F32)
    k.dma("dsync", ones_s[:], ones[:])
    psA = k.ps("psAC", [128, 512])
    psB_ = k.ps("psBC", [128, 512])
    psSt = k.ps("psStC", [1, 512])

    xnm = [k.sb(f"xnm{c}", [128, T], BF) for c in range(16)]
    wxp = [k.sb(f"wxpC{c}", [128, 384], BF) for c in range(16)]
    for c in range(16):
        k.dma("dgps", wxp[c][:], wxpT[c * 128:(c + 1) * 128, :], tag="wxpC")

    with k.scope():
        xf = [k.sb(f"xfC{i}", [128, T], F32) for i in range(2)]
        xq = k.sb("xqC", [128, 512], F32)
        srow = k.sb("srowC", [1, T], F32)
        for n in range(4):
            for m in range(16):
                x_ = xf[m % 2]
                k.dma("dsync", x_[:, 0:512], xs2[m * 128:(m + 1) * 128,
                                                 n * 512:(n + 1) * 512], key=f"xf{m % 2}")
                k.act(xq[:], x_[:, 0:512], AF.Square, tag="xsqC")
                k.mm(psSt[0:1, :], ones_s[:], xq[:], start=(m == 0),
                     stop=(m == 15), tag="xstC")
            k.copy(srow[0:1, n * 512:(n + 1) * 512], psSt[0:1, :], tag="xsr")
        ecol = k.sb("ecolCC", [1, 1], F32)
        k.memset(ecol[0:1, 0:1], 1e-6)
        k.ts(srow[0:1, :], srow[0:1, :], 1.0 / D, None, ALU.mult, tag="xms")
        k.act(srow[0:1, :], srow[0:1, :], AF.Ln, bias=ecol[0:1, 0:1], tag="xln")
        k.act(srow[0:1, :], srow[0:1, :], AF.Exp, scale=-0.5, tag="xrstd")
        k.dma("dsync", rowscr[0:1, :], srow[0:1, :])
        rb = k.sb("rbCC", [128, T], F32)
        k.dma("dsync", rb[:], rowscr[0:1, :].partition_broadcast(128))
        nw_s = k.sb("nwsC", [128, 16], F32)
        for c in range(16):
            k.dma("dsync", nw_s[:, c:c + 1], nw[c * 128:(c + 1) * 128, :])
        for m in range(16):
            x_ = xf[m % 2]
            k.dma("dsync", x_[:], xs2[m * 128:(m + 1) * 128, :], key=f"xf{m % 2}")
            k.stt(xnm[m][:], x_[:], nw_s[:, m:m + 1], rb[:], ALU.mult,
                  ALU.mult, tag="xnC")

    with k.scope():
        mskt = [k.sb(f"mskt{c}", [128, T], BF) for c in range(16)]
        with k.scope():
            wsl = [k.sb(f"wslC{c}", [128, 2048], BF) for c in range(16)]
            for c in range(16):
                k.dma("dgps", wsl[c][:], wselT[c * 128:(c + 1) * 128, :],
                      tag="wslC")
            gcol = k.sb("gcolC", [128, 16], F32)
            bcol = k.sb("bcolC", [128, 16], F32)
            gb = k.sb("gbC", [128, 16], F32)
            for c in range(16):
                k.dma("dsync", gcol[:, c:c + 1], selg[c * 128:(c + 1) * 128, :])
                k.dma("dsync", bcol[:, c:c + 1], selb[c * 128:(c + 1) * 128, :])
            k.tt(gb[:], gcol[:], bcol[:], ALU.mult, tag="gbC")
            for m in range(16):
                for n in range(4):
                    pp = psA if (m * 4 + n) % 2 == 0 else psB_
                    for kk in range(16):
                        k.mm(pp[:], wsl[kk][:, m * 128:(m + 1) * 128],
                             xnm[kk][:, n * 512:(n + 1) * 512],
                             start=(kk == 0), stop=(kk == 15), tag="selmm")
                    k.act(mskt[m][:, n * 512:(n + 1) * 512], pp[:], AF.Sigmoid,
                          bias=gb[:, m:m + 1], scale=gcol[:, m:m + 1], tag="sigC")
        for m in range(16):
            k.tt(xnm[m][:], xnm[m][:], mskt[m][:], ALU.mult, tag="mmulC")

    xpb = [k.sb(f"xpb{c}", [128, T], BF) for c in range(3)]
    with k.scope():
        for m in range(3):
            for n in range(4):
                pp = psA if (m * 4 + n) % 2 == 0 else psB_
                for kk in range(16):
                    k.mm(pp[:], wxp[kk][:, m * 128:(m + 1) * 128],
                         xnm[kk][:, n * 512:(n + 1) * 512],
                         start=(kk == 0), stop=(kk == 15), tag="xpmm")
                k.act(xpb[m][:, n * 512:(n + 1) * 512], pp[:], AF.Copy,
                      tag="xpev")
        k.dma("dgps", Brows[:], xpb[1][:], tag="Bst")
        k.dma("dgps", Crows[:], xpb[2][:], tag="Cst")
    if debug:
        ddel = k.dout("dbg_delta", [128, T], BF)
        k.dma("dgps", ddel[:], xpb[0][:])

    if debug:
        dxn = k.dout("dbg_xnm", [D, T], BF)
        for m in range(16):
            k.dma("dgps", dxn[m * 128:(m + 1) * 128, :], xnm[m][:])
    with k.scope():
        wdt = k.sb("wdtC", [128, 2048], BF)
        k.dma("dgps", wdt[:], wdtT[:], tag="wdtC")
        dtcol = k.sb("dtcolC", [128, 16], F32)
        for c in range(16):
            k.dma("dsync", dtcol[:, c:c + 1], dtb[c * 128:(c + 1) * 128, :])
        a1 = k.sb("a1C", [128, 512], F32)
        a2 = k.sb("a2C", [128, 512], F32)
        dtt = k.sb("dttC", [128, 512], BF)
        st = k.sb("stC", [128, 512], BF)
        for m in range(16):
            for n in range(4):
                pp = psA if (m * 4 + n) % 2 == 0 else psB_
                k.mm(pp[:], wdt[:, m * 128:(m + 1) * 128],
                     xpb[0][:, n * 512:(n + 1) * 512],
                     start=True, stop=True, tag="dtmm")
                if debug and m == 0 and False:
                    dpr_ = k.dout(f"dbg_dtpre{n}", [128, 512], F32)
                    prt = k.sb(f"dprt{n}", [128, 512], F32)
                    k.copy(prt[:], pp[:], tag="dbgdtpre")
                    k.dma("dgps", dpr_[:], prt[:])
                k.act(a1[:], pp[:], AF.Abs, bias=dtcol[:, m:m + 1], tag="spAbs")
                k.act(a1[:], a1[:], AF.Exp, scale=-1.0, tag="spExp")
                k.act(a1[:], a1[:], AF.Ln, bias=1.0, tag="spLn")
                k.act(a2[:], pp[:], AF.Relu, bias=dtcol[:, m:m + 1], tag="spRelu")
                k.tt(dtt[:], a1[:], a2[:], ALU.add, tag="spAdd")
                k.dma("dgps", dtT[m * 128:(m + 1) * 128, n * 512:(n + 1) * 512],
                      dtt[:], tag="dtst", key="dtst")
                k.tt(st[:], dtt[:], xnm[m][:, n * 512:(n + 1) * 512],
                     ALU.mult, tag="sC")
                k.dma("dgps", sfm[m * 128:(m + 1) * 128, n * 512:(n + 1) * 512],
                      st[:], tag="sst", key="sst")
    return k



def build_phase_D():
    k = K("scan")
    dtT = k.din("dtT", [D, T], BF)
    sfm = k.din("sfm", [D, T], BF)
    Brows = k.din("Brows", [128, T], BF)
    Crows = k.din("Crows", [128, T], BF)
    alog = k.din("alog", [D, 128], F32)
    idn = k.din("idn", [128, 128], BF)
    ybf = k.dout("ybf", [D, T], BF)

    idn_s = k.sb("idn_sD", [128, 128], BF)
    k.dma("dsync", idn_s[:], idn[:])
    psY = [k.ps(f"psY{i}", [128, 2048]) for i in range(2)]

    dts = [k.sb(f"dtsD{i}", [128, T], BF) for i in range(2)]
    ss = [k.sb(f"ssD{i}", [128, T], BF) for i in range(2)]
    af = [k.sb(f"afD{i}", [128, 128], F32) for i in range(2)]
    bb = [k.sb(f"bbD{i}", [128, T], BF) for i in range(2)]
    cb = [k.sb(f"cbD{i}", [128, T], BF) for i in range(2)]
    dex = [k.sb(f"dexD{i}", [128, T], BF) for i in range(2)]
    uu = [k.sb(f"uuD{i}", [128, T], BF) for i in range(2)]
    hh = [k.sb(f"hhD{i}", [128, T], BF) for i in range(2)]
    pp_ = [k.sb(f"ppD{i}", [128, T], BF) for i in range(2)]
    yb = k.sb("ybD", [128, T], BF)

    for dcg in range(8):
        for dc in range(2):
            c = dcg * 2 + dc
            k.dma("dsync", dts[dc][:], dtT[c * 128:(c + 1) * 128, :], tag="dtld", key=f"dts{dc}")
            k.dma("dsync", ss[dc][:], sfm[c * 128:(c + 1) * 128, :], tag="sld", key=f"ssd{dc}")
            k.dma("dsync", af[dc][:], alog[c * 128:(c + 1) * 128, :], tag="ald", key=f"afd{dc}")
            k.act(af[dc][:], af[dc][:], AF.Exp, tag="aexp")
            k.ts(af[dc][:], af[dc][:], -1.0, None, ALU.mult, tag="aneg")
        for n in range(128):
            p = n % 2
            k.dma("dgps", bb[p][:], Brows[n:n + 1, :].partition_broadcast(128),
                  tag="bbc", key=f"bb{p}")
            k.dma("dgps", cb[p][:], Crows[n:n + 1, :].partition_broadcast(128),
                  tag="cbc", key=f"cbk{p}")
            for dc in range(2):
                k.act(dex[dc][:], dts[dc][:], AF.Exp,
                      scale=af[dc][:, n:n + 1], tag="dexp")
                k.tt(uu[dc][:], ss[dc][:], bb[p][:], ALU.mult, tag="umul")
                k.tts(hh[dc][:], dex[dc][:], uu[dc][:], 0.0, tag="tts")
                k.tt(pp_[dc][:], hh[dc][:], cb[p][:], ALU.mult, tag="pmul")
                for sl in range(4):
                    k.mm(psY[dc][:, sl * 512:(sl + 1) * 512], idn_s[:],
                         pp_[dc][:, sl * 512:(sl + 1) * 512],
                         start=(n == 0), stop=(n == 127), tag="ymm")
        for dc in range(2):
            c = dcg * 2 + dc
            k.act(yb[:], psY[dc][:], AF.Copy, tag="yev")
            k.dma("dgps", ybf[c * 128:(c + 1) * 128, :], yb[:], tag="yst", key="yst")
    return k



def build_phase_E():
    k = K("l1out")
    xs2 = k.din("xs2", [D, T], F32)
    ybf = k.din("ybf", [D, T], BF)
    dpar = k.din("dpar", [2048, 1], F32)
    woutT = k.din("woutT", [2048, 2048], F32)
    gwT = k.din("gwT", [2048, 5632], F32)
    uwT = k.din("uwT", [2048, 5632], F32)
    dwT = k.din("dwT", [5632, 2048], F32)
    gsc = k.din("gsc", [5632, 1], F32)
    usc = k.din("usc", [5632, 1], F32)
    dsc = k.din("dsc", [2048, 1], F32)
    ones = k.din("ones", [128, 1], F32)

    out = k.dout("out", [D, T], F32)
    upre3 = k.dtmp("upre3", [D, T], F32)
    x3 = k.dtmp("x3", [D, T], F32)
    x3b = k.dtmp("x3b", [D, T], BF)
    upre4 = k.dtmp("upre4", [D, T], F32)
    rowscr = k.dtmp("rowscrE", [4, T], F32)

    ones_s = k.sb("ones_sE", [128, 1], F32)
    k.dma("dsync", ones_s[:], ones[:])

    with k.scope():
        lsr = k.sb("lsrE", [1, T], F32)
        lqr = k.sb("lqrE", [1, T], F32)
        with k.scope():
            psA = k.ps("psAE", [128, 512])
            psB_ = k.ps("psBE", [128, 512])
            psL = k.ps("psLE", [1, 1024])
            zb = [k.sb(f"zbE{c}", [128, T], BF) for c in range(16)]
            wot = [k.sb(f"wotE{c}", [128, T], BF) for c in range(16)]
            dcol = k.sb("dcolE", [128, 16], F32)
            xf = [k.sb(f"xfE{i}", [128, T], F32) for i in range(2)]
            yt = k.sb("ytE", [128, T], BF)
            for c in range(16):
                k.dma("dsync", dcol[:, c:c + 1], dpar[c * 128:(c + 1) * 128, :])
            for c in range(16):
                x_ = xf[c % 2]
                k.dma("dsync", x_[:], xs2[c * 128:(c + 1) * 128, :], key=f"xf{c % 2}")
                k.dma("dgps", yt[:], ybf[c * 128:(c + 1) * 128, :], key="yt")
                k.stt(zb[c][:], x_[:], dcol[:, c:c + 1], yt[:], ALU.mult,
                      ALU.add, tag="zE")
                k.dma("dgps", wot[c][:], woutT[c * 128:(c + 1) * 128, :], tag="woE")
            u_t = k.sb("u_tE", [128, 512], F32)
            usq = k.sb("usqE", [128, 512], F32)
            xres = k.sb("xresE", [128, 512], F32)
            for n in range(4):
                for m in range(16):
                    pp = psA if m % 2 == 0 else psB_
                    for kk in range(16):
                        k.mm(pp[:], wot[kk][:, m * 128:(m + 1) * 128],
                             zb[kk][:, n * 512:(n + 1) * 512],
                             start=(kk == 0), stop=(kk == 15), tag="opmm")
                    k.dma("dgps", xres[:], xs2[m * 128:(m + 1) * 128,
                                               n * 512:(n + 1) * 512], key="xres")
                    k.tt(u_t[:], xres[:], pp[:], ALU.add, tag="resE")
                    k.dma("dgps", upre3[m * 128:(m + 1) * 128,
                                        n * 512:(n + 1) * 512], u_t[:], tag="u3st", key="ust")
                    k.act(usq[:], u_t[:], AF.Square, tag="usqE")
                    k.mm(psL[0:1, 0:512], ones_s[:], u_t[:], start=(m == 0),
                         stop=(m == 15), tag="stSE")
                    k.mm(psL[0:1, 512:1024], ones_s[:], usq[:], start=(m == 0),
                         stop=(m == 15), tag="stQE")
                k.copy(lsr[0:1, n * 512:(n + 1) * 512], psL[0:1, 0:512], tag="lsrE")
                k.copy(lqr[0:1, n * 512:(n + 1) * 512], psL[0:1, 512:1024],
                       tag="lqrE")
        emit_ln_fm(k, lsr, lqr, upre3, x3, x3b, rowscr, eps=1e-5)

    TC = 512
    for tc in range(4):
        t0 = tc * TC
        with k.scope():
            hdn = [k.sb(f"hdnE{i}", [128, TC], BF) for i in range(44)]
            with k.scope():
                psG = k.ps(f"psGE{tc}", [128, 2048])
                psU = k.ps(f"psUE{tc}", [128, 2048])
                wgt = [k.sb(f"wgtE{i}", [128, 512], BF) for i in range(4)]
                wut = [k.sb(f"wutE{i}", [128, 512], BF) for i in range(4)]
                xst = [k.sb(f"xstE{i}", [128, TC], BF) for i in range(4)]
                sg = k.sb("sgE", [128, TC], BF)
                gcol = k.sb("gcolE", [128, 44], F32)
                ucol = k.sb("ucolE", [128, 44], F32)
                for c in range(44):
                    k.dma("dsync", gcol[:, c:c + 1], gsc[c * 128:(c + 1) * 128, :])
                    k.dma("dsync", ucol[:, c:c + 1], usc[c * 128:(c + 1) * 128, :])
                for ig in range(11):
                    for kk in range(16):
                        g_, u_, x_ = wgt[kk % 4], wut[kk % 4], xst[kk % 4]
                        k.dma("dgps", g_[:], gwT[kk * 128:(kk + 1) * 128,
                                                 ig * 512:(ig + 1) * 512],
                              tag="gwE", key=f"wg{kk % 4}")
                        k.dma("dgps", u_[:], uwT[kk * 128:(kk + 1) * 128,
                                                 ig * 512:(ig + 1) * 512],
                              tag="uwE", key=f"wu{kk % 4}")
                        k.dma("dgps", x_[:], x3b[kk * 128:(kk + 1) * 128,
                                                 t0:t0 + TC], tag="xldE", key=f"x{kk % 4}")
                        for j in range(4):
                            k.mm(psG[:, j * TC:(j + 1) * TC],
                                 g_[:, j * 128:(j + 1) * 128], x_[:],
                                 start=(kk == 0), stop=(kk == 15), tag="gmmE")
                            k.mm(psU[:, j * TC:(j + 1) * TC],
                                 u_[:, j * 128:(j + 1) * 128], x_[:],
                                 start=(kk == 0), stop=(kk == 15), tag="ummE")
                    for j in range(4):
                        i = ig * 4 + j
                        k.act(sg[:], psG[:, j * TC:(j + 1) * TC], AF.Silu,
                              scale=gcol[:, i:i + 1], tag="siluE")
                        k.stt(hdn[i][:], psU[:, j * TC:(j + 1) * TC],
                              ucol[:, i:i + 1], sg[:], ALU.mult, ALU.mult,
                              tag="hmulE")
            with k.scope():
                psD = k.ps(f"psDE{tc}", [128, 2048])
                psL2 = k.ps(f"psL2E{tc}", [1, 1024])
                wdt_ = [k.sb(f"wdtE{i}", [128, 512], BF) for i in range(4)]
                dcol2 = k.sb("dcol2E", [128, 16], F32)
                u_t = k.sb("u_tE2", [128, TC], F32)
                usq = k.sb("usqE2", [128, TC], F32)
                xres = k.sb("xresE2", [128, TC], F32)
                lsr = k.sb("lsrE2", [1, TC], F32)
                lqr = k.sb("lqrE2", [1, TC], F32)
                for c in range(16):
                    k.dma("dsync", dcol2[:, c:c + 1], dsc[c * 128:(c + 1) * 128, :])
                for mg in range(4):
                    for kk in range(44):
                        w_ = wdt_[kk % 4]
                        k.dma("dgps", w_[:], dwT[kk * 128:(kk + 1) * 128,
                                                 mg * 512:(mg + 1) * 512],
                              tag="dwE", key=f"wp{kk % 4}")
                        for j in range(4):
                            k.mm(psD[:, j * TC:(j + 1) * TC],
                                 w_[:, j * 128:(j + 1) * 128], hdn[kk][:],
                                 start=(kk == 0), stop=(kk == 43), tag="dmmE")
                    for j in range(4):
                        m = mg * 4 + j
                        k.dma("dgps", xres[:], x3[m * 128:(m + 1) * 128,
                                                  t0:t0 + TC], tag="xrE", key="xres")
                        k.stt(u_t[:], psD[:, j * TC:(j + 1) * TC],
                              dcol2[:, m:m + 1], xres[:], ALU.mult, ALU.add,
                              tag="resE2")
                        k.dma("dgps", upre4[m * 128:(m + 1) * 128, t0:t0 + TC],
                              u_t[:], tag="u4st", key="ust")
                        k.act(usq[:], u_t[:], AF.Square, tag="usqE2")
                        k.mm(psL2[0:1, 0:512], ones_s[:], u_t[:],
                             start=(m == 0), stop=(m == 15), tag="stSE2")
                        k.mm(psL2[0:1, 512:1024], ones_s[:], usq[:],
                             start=(m == 0), stop=(m == 15), tag="stQE2")
                k.copy(lsr[0:1, :], psL2[0:1, 0:512], tag="lsrE2")
                k.copy(lqr[0:1, :], psL2[0:1, 512:1024], tag="lqrE2")
                emit_ln_cols(k, lsr, lqr, upre4, out, None, rowscr, 1e-5,
                             t0, TC, 100 + tc)
    return k


def host_prep_A(inputs, b):
    cos128, ss128 = rope_tables()
    x = inputs["L0_attn_x"]
    return {
        "xT": np.ascontiguousarray(x[b].T),
        "wqd": inputs["L0_attn_wq_down"], "wqu": inputs["L0_attn_wq_up"],
        "wqr": inputs["L0_attn_wq_rope"], "wkv": inputs["L0_attn_wkv_down"],
        "wku": inputs["L0_attn_wk_up"], "wvu": inputs["L0_attn_wv_up"],
        "wkr": inputs["L0_attn_wk_rope"], "wo": inputs["L0_attn_wo"],
        "cosc": cos128, "ssc": ss128, "idn": IDN128, "ones": ONES128,
        "cmask": causal_mask128(),
    }


def host_prep_B(inputs, b, prior):
    return {
        "xs1": prior["xs1"][b], "xs1b": prior["xs1b"][b],
        "wgu": inputs["L0_ffn_w_gate_up"], "wpr": inputs["L0_ffn_w_proj"],
        "ones": ONES128,
    }


def host_prep_C(inputs, b, prior):
    def col(v):
        return np.ascontiguousarray(v.reshape(-1, 1))
    return {
        "xs2": prior["xs2"][b], "xs2b": prior["xs2b"][b],
        "nw": col(inputs["L1_attn_norm_w"]),
        "wselT": np.ascontiguousarray(inputs["L1_attn_select_proj_w"].T),
        "selb": col(inputs["L1_attn_select_proj_b"]),
        "selg": col(inputs["L1_attn_select_gate"]),
        "wxpT": np.ascontiguousarray(inputs["L1_attn_x_proj_w"].T),
        "wdtT": np.ascontiguousarray(inputs["L1_attn_dt_proj_w"].T),
        "dtb": col(inputs["L1_attn_dt_proj_b"]),
        "ones": ONES128,
    }


def host_prep_D(inputs, b, prior):
    return {
        "dtT": prior["dtT"][b], "sfm": prior["sfm"][b],
        "Brows": prior["Brows"][b], "Crows": prior["Crows"][b],
        "alog": inputs["L1_attn_A_log"], "idn": IDN128,
    }


def host_prep_E(inputs, b, prior):
    def col(v):
        return np.ascontiguousarray(v.reshape(-1, 1))
    return {
        "xs2": prior["xs2"][b], "ybf": prior["ybf"][b],
        "dpar": col(inputs["L1_attn_D_param"]),
        "woutT": np.ascontiguousarray(inputs["L1_attn_out_proj_w"].T),
        "gwT": np.ascontiguousarray(inputs["L1_ffn_gate_w"].T),
        "uwT": np.ascontiguousarray(inputs["L1_ffn_up_w"].T),
        "dwT": np.ascontiguousarray(inputs["L1_ffn_down_w"].T),
        "gsc": col(inputs["L1_ffn_gate_s"]), "usc": col(inputs["L1_ffn_up_s"]),
        "dsc": col(inputs["L1_ffn_down_s"]), "ones": ONES128,
    }


PHASES = [
    (build_phase_A, host_prep_A),
    (build_phase_B, host_prep_B),
    (build_phase_C, host_prep_C),
    (build_phase_D, host_prep_D),
    (build_phase_E, host_prep_E),
]


def run_phase(build, prep, inputs, prior, ncore=NCORE):
    k = build()
    in_maps = [prep(inputs, b, prior) if prep is not host_prep_A
               else prep(inputs, b) for b in range(ncore)]
    res = k.run(in_maps)
    out = {}
    for name in k.out_names:
        out[name] = [np.asarray(res.results[b][name]) for b in range(ncore)]
    return out


def kernel(**inputs):
    prior = {}
    for build, prep in PHASES:
        prior.update(run_phase(build, prep, inputs, prior))
    return np.stack([np.ascontiguousarray(o.T) for o in prior["out"]], axis=0)
